# version 01
# baseline (speedup 1.0000x reference)
import numpy as np
import ml_dtypes
from contextlib import ExitStack

import concourse.bass as bass
import concourse.mybir as mybir
from concourse.bass_utils import run_bass_kernel_spmd
from concourse.bass_core_rust import IndirectOffsetOnAxis

F32 = mybir.dt.float32
BF16 = mybir.dt.bfloat16
I32 = mybir.dt.int32
AF = mybir.ActivationFunctionType
ALU = mybir.AluOpType
AX = mybir.AxisListType

BF = ml_dtypes.bfloat16

B, S, D = 4, 4096, 4096
NCHUNK = 16
CH = S // NCHUNK
RPC = 2048
NT = RPC // 128
NH0 = 32
C_SHIFT = 24.0
SCALE = 1.0 / np.sqrt(128.0)
EPS = 1e-5
NEG = -1e9
QG = 512
NQG = RPC // QG

EVEN_CHUNKS = [0, 3, 4, 7, 8, 11, 12, 15]
ODD_CHUNKS = [1, 2, 5, 6, 9, 10, 13, 14]
GROUP_EXT = [8, 16, 24, 32]


def own_chunks(parity):
    return EVEN_CHUNKS if parity == 0 else ODD_CHUNKS


def own_rows(parity):
    ch = own_chunks(parity)
    return np.concatenate([np.arange(c * CH, (c + 1) * CH) for c in ch])


def alibi_slopes(n=32):
    start = 2.0 ** (-2.0 ** (-(np.log2(n) - 3)))
    return np.array([start ** (i + 1) for i in range(n)], dtype=np.float64)


SLOPES = alibi_slopes(32)


def l0_sub_width(h):
    for w in (512, 256, 128, 64, 32):
        span = 1023 if w == 512 else w - 1
        if SLOPES[h] * span <= 38.0:
            return w
    return 32


def l0_keep(h, g, jt):
    i_min = 1024 * g
    gap = max(0, i_min - (128 * jt + 127))
    return SLOPES[h] * gap <= 110.0


def l0_plan(g):
    return [(h, l0_sub_width(h),
             [jt for jt in range(GROUP_EXT[g]) if l0_keep(h, g, jt)])
            for h in range(NH0)]


L0_PLAN = [l0_plan(g) for g in range(4)]
L0_NCALL = [sum(len(jts) * (QG // w) for (_h, w, jts) in L0_PLAN[g])
            for g in range(4)]

OCT_G = np.array([
    [1, 1, 1, 1, 1, 1, 1, 1],
    [-1, 1, 1, -1, 1, -1, -1, 1],
    [-1, -1, 1, 1, 1, 1, -1, -1],
    [-1, 1, -1, 1, 1, -1, 1, -1],
    [-1, -1, -1, -1, 1, 1, 1, 1],
    [-1, 1, -1, 1, -1, 1, -1, 1],
    [-1, 1, 1, -1, -1, 1, 1, -1],
    [-1, -1, 1, 1, -1, -1, 1, 1]], dtype=np.int32)


def group_abs(parity, g):
    ch = own_chunks(parity)
    a, b = ch[2 * g], ch[2 * g + 1]
    return np.concatenate([np.arange(a * CH, (a + 1) * CH),
                           np.arange(b * CH, (b + 1) * CH)])


def host_consts(parity):
    rows = own_rows(parity)
    out = {}
    out["ident"] = np.eye(128, dtype=np.float32)
    out["ones_f"] = np.ones((128, 128), dtype=np.float32)
    out["scat_idx"] = rows.reshape(NT, 128).T.astype(np.int32).copy()
    masks = np.zeros((128, 4 * 8 * 512), dtype=np.float32)
    for g in range(4):
        gabs = group_abs(parity, g)
        ext = GROUP_EXT[g]
        for s_ in range(8):
            jt = ext - 8 + s_
            jabs = 128 * jt + np.arange(128)
            masks[:, (g * 8 + s_) * 512:(g * 8 + s_ + 1) * 512] = (
                gabs[None, :] >= jabs[:, None])
    out["masks"] = masks.astype(BF)
    cols = []
    for g in range(4):
        gabs = group_abs(parity, g)
        for (h, w, jts) in L0_PLAN[g]:
            for jt in jts:
                jabs = 128 * jt + np.arange(128)
                for c0 in range(0, QG, w):
                    Rsub = gabs[c0 + w - 1]
                    if Rsub < 128 * jt:
                        cols.append(np.full(128, NEG, dtype=np.float32))
                    else:
                        cols.append((-C_SHIFT + SLOPES[h] * (jabs - Rsub)
                                     ).astype(np.float32))
    out["bias0"] = np.stack(cols, axis=1)
    assert out["bias0"].shape[1] == sum(L0_NCALL)
    bias1 = np.full((128, 4 * 32), -C_SHIFT, dtype=np.float32)
    ch = own_chunks(parity)
    for g in range(4):
        ext_true = (ch[2 * g + 1] + 1) * 2
        for jt in range(GROUP_EXT[g]):
            if jt >= ext_true:
                bias1[:, g * 32 + jt] = NEG
    out["bias1"] = bias1
    inv = 1.0 / (10000.0 ** (np.arange(0, 128, 2, dtype=np.float64) / 128.0))
    ang_own = rows[None, :].astype(np.float64) * inv[np.arange(128) % 64][:, None]
    out["cos_q"] = np.cos(ang_own).astype(np.float32)
    sq = np.sin(ang_own).astype(np.float32)
    sq[:64] *= -1.0
    out["sin_qs"] = sq
    pos = np.arange(S, dtype=np.float64)
    ang_k = pos[None, :] * inv[(np.arange(256) % 32)][:, None]
    out["cos_k"] = np.cos(ang_k).astype(np.float32)
    out["sin_k"] = np.sin(ang_k).astype(np.float32)
    return out


DEBUG = False
DBGH = 0
L1A_TRUNC = 4

ENGS = ("sy", "ge", "te", "ve", "se")


class Pr:
    def __init__(self, nc, ctx):
        self.nc = nc
        self.ctx = ctx
        self.cnt = {}
        self.sems = {}
        self.events = []
        self.segments = []

    def sem(self, name):
        if name not in self.sems:
            self.sems[name] = self.ctx.enter_context(self.nc.semaphore(name))
            self.cnt[name] = 0
        return self.sems[name]

    def n(self, name):
        self.sem(name)
        return self.cnt[name]

    ESEM = dict(sy="dsy", ge="ge", te="te", ve="ve", se="se")

    def op(self, eng, fn, wait=(), inc=(), dep=False):
        wait = list(wait)
        if dep:
            es = self.ESEM[eng]
            wait.append((es, self.n(es)))
        for s, a in inc:
            self.sem(s)
            self.cnt[s] += a
        for s, _c in wait:
            self.sem(s)
        self.events.append((eng, tuple(wait), fn, tuple(inc)))
        return self.cnt[inc[0][0]] if inc else None

    def dma(self, eng, out, in_, wait=(), sem=None, **kw):
        sem = sem or ("d" + eng)
        return self.op(eng, lambda e: e.dma_start(out=out, in_=in_, **kw),
                       wait=wait, inc=[(sem, 16)])

    def barrier(self):
        self.segments.append(self.events)
        self.events = []

    def build(self):
        self.segments.append(self.events)
        self.events = []
        engmap = dict(sy="sync", ge="gpsimd", te="tensor", ve="vector",
                      se="scalar")
        for si, seg in enumerate(self.segments):
            if si > 0:
                self.nc.all_core_barrier()
            per = {e: [ev for ev in seg if ev[0] == e] for e in ENGS}
            with self.nc.Block() as blk:
                for e in ENGS:
                    evs = per[e]
                    if not evs:
                        continue

                    def body(eng, evs=evs):
                        for (_e, waits, fn, incs) in evs:
                            for s, c in waits:
                                if c > 0:
                                    eng.wait_ge(self.sems[s], c)
                            instr = fn(eng)
                            for s, a in incs:
                                instr.then_inc(self.sems[s], a)
                    getattr(blk, engmap[e])(body)
        self.segments = []

    def sb(self, name, shape, dt):
        return self.ctx.enter_context(self.nc.sbuf_tensor("sb_" + name, shape, dt))

    def ps(self, name, shape, dt=F32):
        return self.ctx.enter_context(self.nc.psum_tensor("ps_" + name, shape, dt))


class Bufs:
    pass


def emit_precast_natural(p, src_f32, dst_bf, nrows, ncols, b):
    for r in range(nrows // 128):
        c = p.dma("ge", b.stage[:, 0:ncols], src_f32[r * 128:(r + 1) * 128, :],
                  wait=[("stg", p.n("stg"))], sem="dge")
        p.dma("ge", dst_bf[r * 128:(r + 1) * 128, :], b.stage[:, 0:ncols],
              wait=[("dge", c)], sem="stg")


def emit_precast_chunked(p, src_f32, dst_flat, n, b):
    nch = n // 128
    for ci in range(nch):
        c = p.dma("ge", b.stage[:, 0:n], src_f32[ci * 128:(ci + 1) * 128, :],
                  wait=[("stg", p.n("stg"))], sem="dge")
        dst = bass.AP(dst_flat.tensor, ci * 16384,
                      [[128, 128], [nch * 16384, nch], [1, 128]])
        p.dma("ge", dst, b.stage[:, 0:n].rearrange("p (co c) -> p co c", c=128),
              wait=[("dge", c)], sem="stg")


def emit_input_ln(p, b, x_src, t):
    c_x = p.dma("sy", b.xt[:], x_src[t * 128:(t + 1) * 128, :],
                wait=[("ve", getattr(b, "ln_apply_ve", 0)),
                      ("se", p.n("se"))], sem="dsy")
    p.op("ve", lambda e: e.reduce_sum(b.s_sum[:], b.xt[:], axis=AX.X),
         wait=[("dsy", c_x)], inc=[("ve", 1)])
    p.op("se", lambda e: e.activation(b.y_sb[:], b.xt[:], AF.Square,
                                      accum_out=b.s_ssq[:]),
         wait=[("dsy", c_x), ("ve", getattr(b, "ysb_free_ve", 0))],
         inc=[("se", 1)])
    b.ln_sq_se = se_sq = p.n("se")
    p.op("ve", lambda e: e.tensor_scalar_mul(b.s_m[:], b.s_sum[:], 1.0 / D),
         inc=[("ve", 1)], dep=True)
    p.op("ve", lambda e: e.tensor_mul(b.s_msq[:], b.s_m[:], b.s_m[:]),
         inc=[("ve", 1)], dep=True)
    p.op("ve", lambda e: e.tensor_scalar(b.s_v[:], b.s_ssq[:], 1.0 / D, EPS,
                                         ALU.mult, ALU.add),
         wait=[("se", se_sq)], inc=[("ve", 1)], dep=True)
    p.op("ve", lambda e: e.tensor_sub(b.s_v[:], b.s_v[:], b.s_msq[:]),
         inc=[("ve", 1)], dep=True)
    ve_v = p.n("ve")
    p.op("se", lambda e: e.activation(b.s_sd[:], b.s_v[:], AF.Sqrt),
         wait=[("ve", ve_v)], inc=[("se", 1)], dep=True)
    se_sd = p.n("se")
    p.op("ve", lambda e: e.reciprocal(b.s_r[:], b.s_sd[:]),
         wait=[("se", se_sd)], inc=[("ve", 1)], dep=True)
    p.op("ve", lambda e: e.tensor_scalar(b.h_tm[:], b.xt[:], b.s_m[:], b.s_r[:],
                                         ALU.subtract, ALU.mult),
         inc=[("ve", 1)], dep=True)
    b.ln_apply_ve = p.n("ve")
    return b.ln_apply_ve


def emit_transpose_tile(p, b, ve_ready, extra_wait=()):
    ev_hist = getattr(b, "tr_evac_hist", [0, 0])
    for grp in range(8):
        par = grp % 2
        tp = b.tp_ps[par]
        w = [("ve", ve_ready), ("ve", ev_hist[-2])] + list(extra_wait)
        for cc in range(4):
            ci = grp * 4 + cc
            p.op("te", lambda e, ci=ci, cc=cc, tp=tp, srct=b.h_tm: e.transpose(
                tp[:, cc * 128:(cc + 1) * 128],
                srct[:, ci * 128:(ci + 1) * 128], b.id_bf[:]),
                 wait=(w if cc == 0 else ()),
                 inc=([("te", 1)] if cc == 3 else ()))
        te_c = p.n("te")
        p.op("ve", lambda e, grp=grp, tp=tp, dst=b.h_fm: e.tensor_copy(
            dst[:, grp * 512:(grp + 1) * 512], tp[:]),
             wait=[("te", te_c)], inc=[("ve", 1)])
        ev_hist.append(p.n("ve"))
    b.tr_evac_hist = ev_hist[-2:]
    return p.n("ve")


def emit_out_ln_residual(p, b, io_x_in, io_x_out, t_glob, n_slices=8,
                         fold=None, direct=False, y_wait=()):
    c_x2 = p.dma("sy", b.xt2[:], io_x_in[t_glob * 128:(t_glob + 1) * 128, :],
                 wait=[("ve", getattr(b, "res_done_ve", 0))], sem="dx2")
    if direct:
        p.op("ve", lambda e: e.reduce_sum(b.s_sum[:], b.y_sb[:], axis=AX.X),
             wait=list(y_wait), inc=[("ve", 1)], dep=True)
        p.op("se", lambda e: e.activation(b.sqd2[:], b.y_sb[:], AF.Square,
                                          accum_out=b.s_ssq[:]),
             wait=list(y_wait) + [("se", getattr(b, "sqd2_se", 0))],
             inc=[("se", 1)])
        b.sqd2_se = se_sq = p.n("se")
    else:
        se_sq = p.n("se")
        p.op("ve", lambda e: e.reduce_sum(b.s_sum[:], b.ys_cols[:, 0:n_slices],
                                          axis=AX.X), inc=[("ve", 1)], dep=True)
        p.op("ve", lambda e: e.reduce_sum(b.s_ssq[:], b.ysq_cols[:, 0:n_slices],
                                          axis=AX.X),
             wait=[("se", se_sq)], inc=[("ve", 1)], dep=True)
    p.op("ve", lambda e: e.tensor_scalar_mul(b.s_m[:], b.s_sum[:], 1.0 / D),
         wait=([("se", se_sq)] if direct else ()), inc=[("ve", 1)], dep=True)
    p.op("ve", lambda e: e.tensor_mul(b.s_msq[:], b.s_m[:], b.s_m[:]),
         inc=[("ve", 1)], dep=True)
    p.op("ve", lambda e: e.tensor_scalar(b.s_v[:], b.s_ssq[:], 1.0 / D,
                                         0.0 if fold else EPS,
                                         ALU.mult, ALU.add),
         inc=[("ve", 1)], dep=True)
    p.op("ve", lambda e: e.tensor_sub(b.s_v[:], b.s_v[:], b.s_msq[:]),
         inc=[("ve", 1)], dep=True)
    if fold is not None:
        sd_bc, sd2_bc = fold
        p.op("ve", lambda e: e.tensor_scalar(b.s_v[:], b.s_v[:], sd2_bc, EPS,
                                             ALU.mult, ALU.add),
             inc=[("ve", 1)], dep=True)
    ve_v = p.n("ve")
    p.op("se", lambda e: e.activation(b.s_sd[:], b.s_v[:], AF.Sqrt),
         wait=[("ve", ve_v)], inc=[("se", 1)], dep=True)
    se_sd = p.n("se")
    p.op("ve", lambda e: e.reciprocal(b.s_r[:], b.s_sd[:]),
         wait=[("se", se_sd)], inc=[("ve", 1)], dep=True)
    if fold is not None:
        sd_bc, sd2_bc = fold
        p.op("ve", lambda e: e.tensor_mul(b.s_r[:], b.s_r[:], sd_bc),
             inc=[("ve", 1)], dep=True)
    p.op("ve", lambda e: e.tensor_scalar(b.y_sb[:], b.y_sb[:], b.s_m[:],
                                         b.s_r[:], ALU.subtract, ALU.mult),
         inc=[("ve", 1)], dep=True)
    b.ysb_free_ve = p.n("ve")
    p.op("ve", lambda e: e.tensor_add(b.xt2[:], b.xt2[:], b.y_sb[:]),
         wait=[("dx2", c_x2)], inc=[("ve", 1)], dep=True)
    ve_res = p.n("ve")
    c_o = p.dma("sy", io_x_out[t_glob * 128:(t_glob + 1) * 128, :], b.xt2[:],
                wait=[("ve", ve_res)], sem="dxo")
    p.op("ve", lambda e: e.engine_nop(), wait=[("dxo", c_o)],
         inc=[("ve", 1)])
    b.res_done_ve = p.n("ve")


def build_l0_attn(p, b, io):
    p.dma("sy", b.id_f[:], io["ident"], sem="dsy")
    p.dma("sy", b.scat[:], io["scat_idx"], sem="dsy")
    c0 = p.dma("sy", b.ones_f[:], io["ones_f"], sem="dsy")
    p.op("ve", lambda e: e.tensor_copy(b.id_bf[:], b.id_f[:]),
         wait=[("dsy", c0)], inc=[("ve", 1)])
    p.op("ve", lambda e: e.tensor_copy(b.ones_bf[:], b.ones_f[:]),
         inc=[("ve", 1)])
    wkv_src = bass.AP(io["Wkv"].tensor, 0, [[256, 128], [128 * 256, 32], [1, 256]])
    c_kv = p.dma("ge", b.wkv[:].rearrange("p (ci c) -> p ci c", c=256), wkv_src,
                 sem="dge")
    emit_precast_chunked(p, io["Wq"], io["wq_scr"], 4096, b)
    emit_precast_natural(p, io["Wo"], io["wo_scr"], 4096, 4096, b)
    c_pc = p.n("stg")

    kv_evac_ve = 0
    scat_hist = 0
    for t in range(NT):
        ve_h = emit_input_ln(p, b, io["x_in"], t)
        ve_tr = emit_transpose_tile(p, b, ve_h,
                                    extra_wait=[("dhs", p.n("dhs"))])
        for ci in range(32):
            p.op("te", lambda e, ci=ci: e.matmul(
                b.kv_ps[:], b.h_fm[:, ci * 128:(ci + 1) * 128],
                b.wkv[:, ci * 256:(ci + 1) * 256],
                start=(ci == 0), stop=(ci == 31)),
                 wait=([("ve", ve_tr), ("dge", c_kv),
                        ("ve", kv_evac_ve)] if ci == 0 else ()),
                 inc=([("te", 1)] if ci == 31 else ()))
        te_kv = p.n("te")
        p.op("ve", lambda e: e.tensor_copy(b.kv_bf[:], b.kv_ps[:]),
             wait=[("te", te_kv), ("gkv", scat_hist)], inc=[("ve", 1)])
        kv_evac_ve = p.n("ve")
        scat_hist = p.op("ge", lambda e, t=t: e.indirect_dma_start(
            out=io["xch0"], out_offset=IndirectOffsetOnAxis(b.scat[:, t:t + 1], 0),
            in_=b.kv_bf[:], in_offset=None),
             wait=[("ve", kv_evac_ve), ("dsy", c0)], inc=[("gkv", 16)])
        p.dma("sy", io["h_scr"][t * 128:(t + 1) * 128, :], b.h_fm[:],
              wait=[("ve", ve_tr)], sem="dhs")
    p.op("ge", lambda e: e.nop(), wait=[("gkv", p.n("gkv")),
                                        ("dhs", p.n("dhs")),
                                        ("stg", c_pc)])
    p.barrier()

    c_k = p.dma("sy", b.k_fm[:], io["xch0"][:, 0:128], transpose=True, sem="dsy")
    for jt in range(32):
        p.dma("sy", b.v_tm[:, jt * 128:(jt + 1) * 128],
              io["xch0"][jt * 128:(jt + 1) * 128, 128:256], sem="dsy")
    c_v = p.n("dsy")

    bias_off = 0
    phase_mark = 0
    last_exp_se = 0
    last_mask_ge = 0
    for g in range(NQG):
        for tt in range(4):
            p.dma("sy", b.h_grp[:, tt * 4096:(tt + 1) * 4096],
                  io["h_scr"][(4 * g + tt) * 128:(4 * g + tt + 1) * 128, :],
                  wait=[("ve", phase_mark)], sem="dhg")
        c_hg = p.n("dhg")
        nb = L0_NCALL[g]
        c_b = p.dma("sy", b.bias_sb[:, 0:nb],
                    io["bias0"][:, bias_off:bias_off + nb],
                    wait=[("se", last_exp_se)], sem="dsy")
        c_m = p.dma("sy", b.mask_sb[:],
                    io["masks"][:, g * 8 * 512:(g + 1) * 8 * 512],
                    wait=[("ge", last_mask_ge)], sem="dsy")
        bias_off += nb

        qmm_te = []
        q_evac_ve = []
        for oc in range(32):
            par = oc % 2
            blk = bass.AP(io["wq_scr"].tensor, oc * 32 * 16384,
                          [[128, 128], [16384, 32], [1, 128]])
            c_w = p.dma("sy", b.wblk[par][:].rearrange("p (ci c) -> p ci c", c=128),
                        blk, wait=[("te", qmm_te[-2] if len(qmm_te) >= 2 else 0),
                                   ("ve", phase_mark)], sem="dwq")
            for ci in range(32):
                rhs = b.h_grp[:, 0:16384].rearrange(
                    "p (tt c) -> p tt c", c=4096)[:, :, ci * 128:(ci + 1) * 128]
                p.op("te", lambda e, ci=ci, par=par, rhs=rhs: e.matmul(
                    b.q_ps[par][:].rearrange("p (tt c) -> p tt c", c=128),
                    b.wblk[par][:, ci * 128:(ci + 1) * 128],
                    rhs, start=(ci == 0), stop=(ci == 31)),
                     wait=([("dwq", c_w), ("dhg", c_hg),
                            ("ve", q_evac_ve[-2] if len(q_evac_ve) >= 2 else 0)]
                           if ci == 0 else ()),
                     inc=([("te", 1)] if ci == 31 else ()))
            qmm_te.append(p.n("te"))
            p.op("ve", lambda e, oc=oc, par=par: e.tensor_copy(
                b.q_sb[:, oc * 512:(oc + 1) * 512], b.q_ps[par][:]),
                 wait=[("te", qmm_te[-1])], inc=[("ve", 1)])
            q_evac_ve.append(p.n("ve"))
        ve_q = q_evac_ve[-1]

        call_idx = 0
        for hi_, (h, w_h, jts) in enumerate(L0_PLAN[g]):
            sex_hist = []
            av_hist = []
            ve_tail = p.n("ve")
            for jj, jt in enumerate(jts):
                par = jj % 2
                last = jj == len(jts) - 1
                if jj >= 2:
                    w_s = [("se", sex_hist[jj - 2])]
                else:
                    w_s = [("ve", ve_q), ("dsy", c_k), ("se", last_exp_se)]
                p.op("te", lambda e, jt=jt, h=h, par=par: e.matmul(
                    b.s_ps[par][:], b.k_fm[:, jt * 128:(jt + 1) * 128],
                    b.q_sb[:, h * 512:(h + 1) * 512], start=True, stop=True),
                     wait=w_s, inc=[("te", 1)])
                te_s = p.n("te")
                ptp = b.pt[par]
                nsub = QG // w_h
                for si in range(nsub):
                    col = call_idx
                    call_idx += 1
                    c0_ = si * w_h
                    w_e = []
                    if si == 0:
                        w_e = [("te", te_s), ("dsy", c_b)]
                        if jj >= 2:
                            w_e.append(("te", av_hist[jj - 2]))
                        else:
                            w_e.append(("ve", ve_tail))
                    p.op("se", lambda e, ptp=ptp, c0_=c0_, w_h=w_h, col=col,
                         par=par: e.activation(
                             ptp[:, c0_:c0_ + w_h], b.s_ps[par][:, c0_:c0_ + w_h],
                             AF.Exp, bias=b.bias_sb[:, col:col + 1], scale=SCALE),
                         wait=w_e, inc=[("se", 1)])
                sex_hist.append(p.n("se"))
                last_exp_se = sex_hist[-1]
                slot = jt - (GROUP_EXT[g] - 8)
                if slot >= 0:
                    p.op("ge", lambda e, ptp=ptp, slot=slot: e.tensor_mul(
                        ptp[:], ptp[:],
                        b.mask_sb[:, slot * 512:(slot + 1) * 512]),
                         wait=[("se", sex_hist[-1]), ("dsy", c_m)],
                         inc=[("ge", 1)])
                    last_mask_ge = p.n("ge")
                    dep = [("ge", last_mask_ge)]
                else:
                    dep = [("se", sex_hist[-1])]
                p.op("te", lambda e, ptp=ptp, first=(jj == 0), last=last:
                     e.matmul(b.l_ps[:], b.ones_bf[:, 0:1], ptp[:],
                              start=first, stop=last),
                     wait=dep, inc=[("te", 1)])
                p.op("te", lambda e, ptp=ptp, jt=jt, first=(jj == 0),
                     last=last: e.matmul(
                         b.av_ps[:], b.v_tm[:, jt * 128:(jt + 1) * 128], ptp[:],
                         start=first, stop=last),
                     wait=([("dsy", c_v)] if jj == 0 else ()),
                     inc=[("te", 1)])
                av_hist.append(p.n("te"))
            te_done = p.n("te")
            p.op("ve", lambda e: e.tensor_scalar(
                b.l_sb[0:1, :], b.l_ps[:], 1e-35, None, ALU.add),
                 wait=[("te", te_done)], inc=[("ve", 1)])
            p.op("ve", lambda e: e.reciprocal(b.rl_sb[0:1, :], b.l_sb[0:1, :]),
                 inc=[("ve", 1)], dep=True)
            ve_rl = p.n("ve")
            p.op("te", lambda e: e.matmul(
                b.bc_ps[:], b.ones_f[0:1, :], b.rl_sb[0:1, :],
                start=True, stop=True),
                 wait=[("ve", ve_rl)], inc=[("te", 1)])
            te_bc = p.n("te")
            p.op("ve", lambda e: e.tensor_copy(b.bc_sb[:], b.bc_ps[:]),
                 wait=[("te", te_bc)], inc=[("ve", 1)])
            p.op("ve", lambda e, h=h: e.scalar_tensor_tensor(
                b.attn_sb[:, h * 512:(h + 1) * 512], b.av_ps[:], 1.0,
                b.bc_sb[:], ALU.mult, ALU.mult),
                 inc=[("ve", 1)], dep=True)
            if DEBUG and g == 0 and hi_ == DBGH:
                vv = p.n("ve")
                p.dma("ge", io["dbg_l"], b.l_sb[0:1, :], wait=[("ve", vv)], sem="dbg")
                p.dma("ge", io["dbg_rl"], b.rl_sb[0:1, :], wait=[("ve", vv)], sem="dbg")
                p.dma("ge", io["dbg_bc"], b.bc_sb[:], wait=[("ve", vv)], sem="dbg")
                p.dma("ge", io["dbg_pt"], b.pt[(len(jts) - 1) % 2][:],
                      wait=[("ve", vv)], sem="dbg")
        ve_attn = p.n("ve")
        if DEBUG and g == 0:
            p.dma("ge", io["dbg_q"], b.q_sb[:], wait=[("ve", ve_attn)], sem="dbg")
            p.dma("ge", io["dbg_attn"], b.attn_sb[:], wait=[("ve", ve_attn)], sem="dbg")

        y_evac_ve = [0, 0]
        for ts in range(4):
            t_glob = 4 * g + ts
            for os_ in range(8):
                batch_hist = []
                cws = []
                ymm_hist = []
                for k in range(2):
                    base = (4 * k * 128) * 4096 + os_ * 512
                    src = bass.AP(io["wo_scr"].tensor, base,
                                  [[4096, 128], [524288, 4], [1, 512]])
                    w_b = getattr(b, "wob_free_te", [0, 0])[k]
                    cw = p.dma("se", b.wob[k][:].rearrange(
                        "p (dj c) -> p dj c", c=512), src,
                        wait=[("te", w_b)], sem="dwo")
                    cws.append(cw)
                for dc in range(32):
                    kb = dc // 4
                    if dc % 4 == 0 and kb >= 2:
                        base = (4 * kb * 128) * 4096 + os_ * 512
                        src = bass.AP(io["wo_scr"].tensor, base,
                                      [[4096, 128], [524288, 4], [1, 512]])
                        cw = p.dma("se", b.wob[kb % 2][:].rearrange(
                            "p (dj c) -> p dj c", c=512), src,
                            wait=[("te", batch_hist[kb - 2])], sem="dwo")
                        cws.append(cw)
                    w_y = [("dwo", cws[kb])]
                    if dc == 0:
                        w_y += [("ve", ve_attn),
                                ("ve", y_evac_ve[-2])]
                    p.op("te", lambda e, dc=dc, ts=ts, par=(ts * 8 + os_) % 2:
                         e.matmul(
                             b.y_ps[par][:],
                             b.attn_sb[:, dc * 512 + ts * 128:
                                       dc * 512 + (ts + 1) * 128],
                             b.wob[(dc // 4) % 2][:, (dc % 4) * 512:
                                                  (dc % 4 + 1) * 512],
                             start=(dc == 0), stop=(dc == 31)),
                         wait=w_y, inc=[("te", 1)])
                    if dc % 4 == 3:
                        batch_hist.append(p.n("te"))
                b.wob_free_te = [batch_hist[-2], batch_hist[-1]]
                te_y = p.n("te")
                p.op("ve", lambda e, os_=os_, par=(ts * 8 + os_) % 2:
                     e.tensor_copy(b.y_sb[:, os_ * 512:(os_ + 1) * 512],
                                   b.y_ps[par][:]),
                     wait=[("te", te_y), ("se", getattr(b, "wsq_se", 0))],
                     inc=[("ve", 1)])
                y_evac_ve.append(p.n("ve"))
                ve_yc = p.n("ve")
                p.op("ve", lambda e, os_=os_: e.reduce_sum(
                    b.ys_cols[:, os_:os_ + 1],
                    b.y_sb[:, os_ * 512:(os_ + 1) * 512], axis=AX.X),
                     inc=[("ve", 1)], dep=True)
                p.op("se", lambda e, os_=os_: e.activation(
                    b.sqd[:], b.y_sb[:, os_ * 512:(os_ + 1) * 512],
                    AF.Square, accum_out=b.ysq_cols[:, os_:os_ + 1]),
                     wait=[("ve", ve_yc)], inc=[("se", 1)])
                b.wsq_se = p.n("se")
            if DEBUG and g == 0 and ts == 0:
                p.dma("ge", io["dbg_y"], b.y_sb[:],
                      wait=[("ve", p.n("ve")), ("se", p.n("se"))], sem="dbg")
            emit_out_ln_residual(p, b, io["x_in"], io["x_out"], t_glob)
        phase_mark = p.n("ve")
    if DEBUG:
        p.dma("ge", io["dbg_h"], io["h_scr"], sem="dbg")
        p.dma("ge", io["dbg_kv"], io["xch0"], sem="dbg")
        p.op("ge", lambda e: e.nop(), wait=[("dbg", p.n("dbg"))])
    p.op("sy", lambda e: e.nop(), wait=[("dxo", p.n("dxo"))])


def build_l0_ffn(p, b, io):
    c0 = p.dma("sy", b.id_f[:], io["ident"], sem="dsy")
    c1 = p.dma("sy", b.ones_f[:], io["ones_f"], sem="dsy")
    p.op("ve", lambda e: e.tensor_copy(b.id_bf[:], b.id_f[:]),
         wait=[("dsy", c1)], inc=[("ve", 1)])
    c_fb = p.dma("sy", b.frow[0:1, :], io["fbias"].rearrange("(a d) -> a d", a=1),
                 sem="dsy")
    for os_ in range(8):
        p.op("te", lambda e, os_=os_: e.matmul(
            b.bc_ps[:], b.ones_f[0:1, :], b.frow[0:1, os_ * 512:(os_ + 1) * 512],
            start=True, stop=True),
             wait=[("dsy", c_fb), ("ve", p.n("ve"))], inc=[("te", 1)])
        p.op("ve", lambda e, os_=os_: e.tensor_copy(
            b.bias_bc[:, os_ * 512:(os_ + 1) * 512], b.bc_ps[:]),
             wait=[("te", p.n("te"))], inc=[("ve", 1)])
    ve_bb = p.n("ve")

    dq_ve = 0
    for oc in range(32):
        c_qv = p.dma("ge", b.qvf[:],
                     io["q_vals"].rearrange("o n j -> o (n j)")[
                         oc * 128:(oc + 1) * 128, :],
                     wait=[("ve", dq_ve)], sem="dge")
        c_qs = p.dma("sy", b.qsc[:], io["q_scales"][oc * 128:(oc + 1) * 128, :],
                     wait=[("ve", dq_ve)], sem="dqs")
        sc_b = bass.AP(b.qsc, 0, [[128, 128], [1, 128], [0, 32]])
        p.op("ve", lambda e, sc_b=sc_b: e.scalar_tensor_tensor(
            b.wrow[:].rearrange("p (n j) -> p n j", j=32),
            b.qvf[:].rearrange("p (n j) -> p n j", j=32), 128.0, sc_b,
            ALU.subtract, ALU.mult),
             wait=[("dge", c_qv), ("dqs", c_qs), ("te", p.n("te"))],
             inc=[("ve", 1)])
        ve_w = p.n("ve")
        for grp in range(8):
            par = grp % 2
            tp = b.tp_ps[par]
            for cc in range(4):
                ci = grp * 4 + cc
                p.op("te", lambda e, ci=ci, cc=cc, tp=tp: e.transpose(
                    tp[:, cc * 128:(cc + 1) * 128],
                    b.wrow[:, ci * 128:(ci + 1) * 128], b.id_bf[:]),
                     wait=([("ve", ve_w), ("ve", b.trw_hist[-2])]
                           if cc == 0 else ()),
                     inc=([("te", 1)] if cc == 3 else ()))
            te_c = p.n("te")
            p.op("ve", lambda e, grp=grp, tp=tp: e.tensor_copy(
                b.wtr[:, grp * 512:(grp + 1) * 512], tp[:]),
                 wait=[("te", te_c), ("dwr", p.n("dwr"))], inc=[("ve", 1)])
            b.trw_hist.append(p.n("ve"))
        dq_ve = p.n("ve")
        dst = bass.AP(io["wdeq"].tensor, oc * 128,
                      [[4096, 128], [128 * 4096, 32], [1, 128]])
        p.dma("sy", dst, b.wtr[:].rearrange("p (ci c) -> p ci c", c=128),
              wait=[("ve", dq_ve)], sem="dwr")

    for t in range(NT):
        ve_h = emit_input_ln(p, b, io["x_in"], t)
        ve_tr = emit_transpose_tile(p, b, ve_h,
                                    extra_wait=[("dhs", p.n("dhs"))])
        p.dma("sy", io["h_scr"][t * 128:(t + 1) * 128, :], b.h_fm[:],
              wait=[("ve", ve_tr)], sem="dhs")
    c_h = p.n("dhs")
    c_wd = p.n("dwr")

    ymm_hist = [0, 0]
    for os_ in range(8):
        srcw = bass.AP(io["wdeq"].tensor, os_ * 512,
                       [[4096, 128], [128 * 4096, 32], [1, 512]])
        c_w = p.dma("sy", b.wos[:].rearrange("p (ci c) -> p ci c", c=512), srcw,
                    wait=[("dwr", c_wd), ("te", ymm_hist[-1])], sem="dwq")
        for t in range(NT):
            par = t % 2
            c_ht = p.dma("se", b.hblk[par][:],
                         io["h_scr"][t * 128:(t + 1) * 128, :],
                         wait=[("dhs", c_h), ("te", ymm_hist[-2])], sem="dhb")
            for ci in range(32):
                p.op("te", lambda e, ci=ci, par=par: e.matmul(
                    b.y_ps[par][:],
                    b.hblk[par][:, ci * 128:(ci + 1) * 128],
                    b.wos[:, ci * 512:(ci + 1) * 512],
                    start=(ci == 0), stop=(ci == 31)),
                     wait=([("dwq", c_w), ("dhb", c_ht),
                            ("ve", b.yev_hist[-2])] if ci == 0 else ()),
                     inc=([("te", 1)] if ci == 31 else ()))
            ymm_hist.append(p.n("te"))
            p.op("ve", lambda e, os_=os_, par=par: e.scalar_tensor_tensor(
                b.ysl[:], b.y_ps[par][:], 1.0,
                b.bias_bc[:, os_ * 512:(os_ + 1) * 512], ALU.mult, ALU.add),
                 wait=[("te", ymm_hist[-1]), ("ve", ve_bb),
                       ("dys", p.n("dys"))], inc=[("ve", 1)])
            b.yev_hist.append(p.n("ve"))
            p.dma("sy", io["y_scr"][t * 128:(t + 1) * 128,
                                    os_ * 512:(os_ + 1) * 512], b.ysl[:],
                  wait=[("ve", b.yev_hist[-1])], sem="dys")
    c_y = p.n("dys")

    for t in range(NT):
        c_yt = p.dma("sy", b.y_sb[:], io["y_scr"][t * 128:(t + 1) * 128, :],
                     wait=[("dys", c_y), ("ve", getattr(b, "ysb_free_ve", 0)),
                           ("se", getattr(b, "sqd2_se", 0))], sem="dyt")
        emit_out_ln_residual(p, b, io["x_in"], io["x_out"], t, direct=True,
                             y_wait=[("dyt", c_yt)])
    p.op("sy", lambda e: e.nop(), wait=[("dxo", p.n("dxo"))])


def alloc_l0f_bufs(p):
    nc = p.nc
    b = Bufs()
    b.id_f = p.sb("id_f", [128, 128], F32)
    b.id_bf = p.sb("id_bf", [128, 128], BF16)
    b.ones_f = p.sb("ones_f", [128, 128], F32)
    b.frow = p.sb("frow", [1, 4096], F32)
    b.bias_bc = p.sb("bias_bc", [128, 4096], F32)
    b.xt = p.sb("xt", [128, 4096], F32)
    b.xt2 = p.sb("xt2", [128, 4096], F32)
    b.y_sb = p.sb("y_sb", [128, 4096], F32)
    b.sqd2 = b.xt
    b.h_tm = p.sb("h_tm", [128, 4096], BF16)
    b.h_fm = p.sb("h_fm", [128, 4096], BF16)
    b.qvf = p.sb("qvf", [128, 4096], F32)
    b.qsc = p.sb("qsc", [128, 128], F32)
    b.wrow = p.sb("wrow", [128, 4096], BF16)
    b.wtr = p.sb("wtr", [128, 4096], BF16)
    b.wos = p.sb("wos", [128, 32 * 512], BF16)
    b.hblk = [p.sb(f"hblk{i}", [128, 4096], BF16) for i in range(2)]
    b.ysl = p.sb("ysl", [128, 512], F32)
    for nme in ["s_sum", "s_ssq", "s_m", "s_msq", "s_v", "s_sd", "s_r"]:
        setattr(b, nme, p.sb(nme, [128, 1], F32))
    b.trw_hist = [0, 0]
    b.yev_hist = [0, 0]
    b.tp_ps = [p.ps("tp0", [128, 512], BF16), p.ps("tp1", [128, 512], BF16)]
    b.bc_ps = p.ps("bc_ps", [128, 512])
    b.y_ps = [p.ps("y_ps0", [128, 512]), p.ps("y_ps1", [128, 512])]
    return b


def build_launch_l0f():
    nc = bass.Bass("TRN2", target_bir_lowering=False, num_devices=8)
    ctx = ExitStack()
    io = {}
    io["x_in"] = nc.dram_tensor("x_in", [RPC, D], F32, kind="ExternalInput").ap()
    io["q_scales"] = nc.dram_tensor("q_scales", [4096, 128], F32, kind="ExternalInput").ap()
    io["q_vals"] = nc.dram_tensor("q_vals", [4096, 128, 32], I32, kind="ExternalInput").ap()
    io["fbias"] = nc.dram_tensor("fbias", [4096], F32, kind="ExternalInput").ap()
    io["ident"] = nc.dram_tensor("ident", [128, 128], F32, kind="ExternalInput").ap()
    io["ones_f"] = nc.dram_tensor("ones_f", [128, 128], F32, kind="ExternalInput").ap()
    io["x_out"] = nc.dram_tensor("x_out", [RPC, D], F32, kind="ExternalOutput").ap()
    io["wdeq"] = nc.dram_tensor("wdeq", [D * D], BF16, kind="Internal").ap()
    io["h_scr"] = nc.dram_tensor("h_scr", [RPC, D], BF16, kind="Internal").ap()
    io["y_scr"] = nc.dram_tensor("y_scr", [RPC, D], F32, kind="Internal").ap()
    with ctx:
        p = Pr(nc, ctx)
        b = alloc_l0f_bufs(p)
        build_l0_ffn(p, b, io)
        p.build()
    return nc


def l0f_in_map(inp, c, consts):
    return dict(
        q_scales=inp["L0_ffn_q_scales"], q_vals=inp["L0_ffn_q_vals"],
        fbias=inp["L0_ffn_bias"],
        ident=consts["ident"], ones_f=consts["ones_f"])


L1_EXT = GROUP_EXT


def build_l1_attn(p, b, io):
    p.dma("sy", b.id_f[:], io["ident"], sem="dsy")
    p.dma("sy", b.scat[:], io["scat_idx"], sem="dsy")
    c0 = p.dma("sy", b.ones_f[:], io["ones_f"], sem="dsy")
    c_b1 = p.dma("sy", b.bias_sb[:, 0:128], io["bias1"], sem="dsy")
    p.op("ve", lambda e: e.tensor_copy(b.id_bf[:], b.id_f[:]),
         wait=[("dsy", c0)], inc=[("ve", 1)])
    p.op("ve", lambda e: e.tensor_copy(b.ones_bf[:], b.ones_f[:]),
         inc=[("ve", 1)])
    wcat_src = bass.AP(io["wkrc"].tensor, 0,
                       [[1024, 128], [128 * 1024, 32], [1, 1024]])
    c_wc = p.dma("ge", b.wcat[:].rearrange("p (ci c) -> p ci c", c=1024),
                 wcat_src, sem="dge")
    emit_precast_chunked(p, io["wq_t"], io["wq_scr"], 4096, b)
    emit_precast_natural(p, io["wo_t"], io["wo_scr"], 4096, 4096, b)
    c_pc = p.n("stg")

    krc_evac_ve = 0
    scat_hist = 0
    for t in range(NT):
        ve_h = emit_input_ln(p, b, io["x_in"], t)
        ve_tr = emit_transpose_tile(p, b, ve_h,
                                    extra_wait=[("dhs", p.n("dhs"))])
        for half in range(2):
            for ci in range(32):
                p.op("te", lambda e, ci=ci, half=half: e.matmul(
                    b.q_ps[half][:], b.h_fm[:, ci * 128:(ci + 1) * 128],
                    b.wcat[:, ci * 1024 + half * 512:
                           ci * 1024 + (half + 1) * 512],
                    start=(ci == 0), stop=(ci == 31)),
                     wait=([("ve", ve_tr), ("dge", c_wc),
                            ("ve", krc_evac_ve)] if ci == 0 else ()),
                     inc=([("te", 1)] if ci == 31 else ()))
        te_kv = p.n("te")
        for half in range(2):
            p.op("ve", lambda e, half=half: e.tensor_copy(
                b.krc_bf[:, half * 512:(half + 1) * 512], b.q_ps[half][:]),
                 wait=[("te", te_kv), ("gkv", scat_hist)], inc=[("ve", 1)])
        krc_evac_ve = p.n("ve")
        scat_hist = p.op("ge", lambda e, t=t: e.indirect_dma_start(
            out=io["xch1"], out_offset=IndirectOffsetOnAxis(b.scat[:, t:t + 1], 0),
            in_=b.krc_bf[:], in_offset=None),
             wait=[("ve", krc_evac_ve), ("dsy", c0)], inc=[("gkv", 16)])
        p.dma("sy", io["h_scr"][t * 128:(t + 1) * 128, :], b.h_fm[:],
              wait=[("ve", ve_tr)], sem="dhs")
    p.op("ge", lambda e: e.nop(), wait=[("gkv", p.n("gkv")),
                                        ("dhs", p.n("dhs")),
                                        ("stg", c_pc)])
    if L1A_TRUNC < 2:
        p.dma("sy", io["x_out"], io["x_in"], sem="dxo")
        p.op("sy", lambda e: e.nop(), wait=[("dxo", p.n("dxo"))])
        return
    p.barrier()

    for ci in range(4):
        p.dma("sy", b.c_fm[:, ci * 4096:(ci + 1) * 4096],
              io["xch1"][:, 512 + ci * 128:512 + (ci + 1) * 128],
              transpose=True, sem="dsy")
    for ci in range(4):
        p.dma("sy", b.kr_fm[:, ci * 4096:(ci + 1) * 4096],
              io["xch1"][:, ci * 128:(ci + 1) * 128],
              transpose=True, sem="dsy")
    c_tr = p.n("dsy")
    upk_src = bass.AP(io["wupk_t"].tensor, 0,
                      [[512, 128], [128 * 512, 4], [1, 512]])
    c_uk = p.dma("ge", b.wupk[:].rearrange("p (ci c) -> p ci c", c=512),
                 upk_src, sem="dge")
    upv_src = bass.AP(io["wupv_t"].tensor, 0,
                      [[1024, 128], [128 * 1024, 4], [1, 1024]])
    c_uv = p.dma("ge", b.wupv[:].rearrange("p (ci c) -> p ci c", c=1024),
                 upv_src, sem="dge")
    st_hist = 0
    for c in range(2):
        cc = p.dma("ge", b.kcos[:], io["cos_k"][c * 128:(c + 1) * 128, :],
                   wait=[("ve", p.n("ve"))], sem="dck")
        cs = p.dma("ge", b.ksin[:], io["sin_k"][c * 128:(c + 1) * 128, :],
                   sem="dck")
        a_sl = b.kr_fm[:, c * 4096:(c + 1) * 4096]
        b_sl = b.kr_fm[:, (c + 2) * 4096:(c + 3) * 4096]
        p.op("ve", lambda e, a_sl=a_sl: e.tensor_mul(b.rt1[:], a_sl, b.kcos[:]),
             wait=[("dck", cs), ("dsy", c_tr), ("kst", st_hist)],
             inc=[("ve", 1)])
        p.op("ve", lambda e, b_sl=b_sl: e.tensor_mul(b.rt2[:], b_sl, b.ksin[:]),
             inc=[("ve", 1)])
        p.op("ve", lambda e: e.tensor_sub(b.rt1[:], b.rt1[:], b.rt2[:]),
             inc=[("ve", 1)], dep=True)
        p.op("ve", lambda e, b_sl=b_sl: e.tensor_mul(b.rt2[:], b_sl, b.kcos[:]),
             inc=[("ve", 1)], dep=True)
        p.op("ve", lambda e, a_sl=a_sl, b_sl=b_sl: e.tensor_mul(
            b_sl, a_sl, b.ksin[:]), inc=[("ve", 1)], dep=True)
        p.op("ve", lambda e, b_sl=b_sl: e.tensor_add(b.rt2[:], b.rt2[:], b_sl),
             inc=[("ve", 1)], dep=True)
        ve_rot = p.n("ve")
        for gg in range(4):
            g = c * 4 + gg
            st_hist = p.dma("ge", io["k_stage"][g * 128:g * 128 + 32, :],
                            b.rt1[gg * 32:(gg + 1) * 32, :],
                            wait=[("ve", ve_rot)], sem="kst")
            st_hist = p.dma("ge", io["k_stage"][g * 128 + 64:g * 128 + 96, :],
                            b.rt2[gg * 32:(gg + 1) * 32, :], sem="kst")
    kc_ev = 0
    for ns in range(4):
        for tt in range(8):
            for ci in range(4):
                p.op("te", lambda e, ns=ns, tt=tt, ci=ci: e.matmul(
                    b.av_ps[:], b.wupk[:, ci * 512 + ns * 128:
                                       ci * 512 + (ns + 1) * 128],
                    b.c_fm[:, ci * 4096 + tt * 512:ci * 4096 + (tt + 1) * 512],
                    start=(ci == 0), stop=(ci == 3)),
                     wait=([("dge", c_uk), ("dsy", c_tr),
                            ("ve", kc_ev)] if ci == 0 else ()),
                     inc=([("te", 1)] if ci == 3 else ()))
            p.op("ve", lambda e, tt=tt: e.tensor_copy(
                b.kc_bf[:, tt * 512:(tt + 1) * 512], b.av_ps[:]),
                 wait=[("te", p.n("te")), ("kst", p.n("kst"))],
                 inc=[("ve", 1)])
            kc_ev = p.n("ve")
        g0 = 2 * ns
        for pc in range(4):
            g = g0 + (0 if pc < 2 else 1)
            t0 = (pc % 2) * 32
            row = g * 128 + (32 if pc % 2 == 0 else 96)
            st_hist = p.dma("ge", io["k_stage"][row:row + 32, :],
                            b.kc_bf[pc * 32:(pc + 1) * 32, :],
                            wait=[("ve", kc_ev)], sem="kst")
    vv_ev = 0
    for g in range(8):
        for jt in range(32):
            for ci in range(4):
                p.op("te", lambda e, g=g, jt=jt, ci=ci: e.matmul(
                    b.lbc_ps[0:128, 0:128],
                    b.c_fm[:, ci * 4096 + jt * 128:ci * 4096 + (jt + 1) * 128],
                    b.wupv[:, ci * 1024 + g * 128:ci * 1024 + (g + 1) * 128],
                    start=(ci == 0), stop=(ci == 3)),
                     wait=([("dge", c_uv), ("ve", vv_ev)] if ci == 0 else ()),
                     inc=([("te", 1)] if ci == 3 else ()))
            p.op("ve", lambda e: e.tensor_copy(b.vv_bf[:], b.lbc_ps[0:128, 0:128]),
                 wait=[("te", p.n("te")), ("kst", p.n("kst"))], inc=[("ve", 1)])
            vv_ev = p.n("ve")
            p.dma("ge", io["v_stage"][g * 4096 + jt * 128:
                                      g * 4096 + (jt + 1) * 128, :],
                  b.vv_bf[:], wait=[("ve", vv_ev)], sem="kst")
    p.op("ge", lambda e: e.nop(), wait=[("kst", p.n("kst"))])
    ph2_ve = p.n("ve")
    ph2_kst = p.n("kst")
    if L1A_TRUNC < 3:
        p.dma("sy", io["x_out"], io["x_in"], wait=[("kst", ph2_kst)], sem="dxo")
        p.op("sy", lambda e: e.nop(), wait=[("dxo", p.n("dxo"))])
        return
    cqs = p.dma("ge", b.cos_q[:], io["cos_q"], wait=[("ve", ph2_ve)], sem="dge")
    p.dma("ge", b.sin_qs[:], io["sin_qs"], wait=[("kst", ph2_kst)], sem="dge")

    phase_mark = ph2_ve
    last_exp_se = 0
    last_mask_ge = 0
    for g in range(NQG):
        ext = L1_EXT[g]
        for tt in range(4):
            p.dma("sy", b.h_grp[:, tt * 4096:(tt + 1) * 4096],
                  io["h_scr"][(4 * g + tt) * 128:(4 * g + tt + 1) * 128, :],
                  wait=[("ve", phase_mark)], sem="dhg")
        c_hg = p.n("dhg")
        c_m = p.dma("sy", b.mask_sb[:],
                    io["masks"][:, g * 8 * 512:(g + 1) * 8 * 512],
                    wait=[("ge", last_mask_ge), ("ve", phase_mark),
                          ("kst", ph2_kst)], sem="dsy")

        qmm_te = []
        q_evac_ve = []
        for oc in range(32):
            par = oc % 2
            blk = bass.AP(io["wq_scr"].tensor, oc * 32 * 16384,
                          [[128, 128], [16384, 32], [1, 128]])
            c_w = p.dma("sy", b.wblk[par][:].rearrange("p (ci c) -> p ci c", c=128),
                        blk, wait=[("te", qmm_te[-2] if len(qmm_te) >= 2 else 0),
                                   ("ve", phase_mark)], sem="dwq")
            for ci in range(32):
                rhs = b.h_grp[:, 0:16384].rearrange(
                    "p (tt c) -> p tt c", c=4096)[:, :, ci * 128:(ci + 1) * 128]
                p.op("te", lambda e, ci=ci, par=par, rhs=rhs: e.matmul(
                    b.q_ps[par][:].rearrange("p (tt c) -> p tt c", c=128),
                    b.wblk[par][:, ci * 128:(ci + 1) * 128],
                    rhs, start=(ci == 0), stop=(ci == 31)),
                     wait=([("dwq", c_w), ("dhg", c_hg),
                            ("ve", q_evac_ve[-2] if len(q_evac_ve) >= 2 else 0)]
                           if ci == 0 else ()),
                     inc=([("te", 1)] if ci == 31 else ()))
            qmm_te.append(p.n("te"))
            p.op("ve", lambda e, oc=oc, par=par: e.tensor_copy(
                b.q_sb[:, oc * 512:(oc + 1) * 512], b.q_ps[par][:]),
                 wait=[("te", qmm_te[-1])], inc=[("ve", 1)])
            q_evac_ve.append(p.n("ve"))
        csl = b.cos_q[:, g * 512:(g + 1) * 512]
        ssl = b.sin_qs[:, g * 512:(g + 1) * 512]
        rope_hist = [0, 0]
        swap_hist = []
        for hh in range(32):
            par = hh % 2
            qsl = b.q_sb[:, hh * 512:(hh + 1) * 512]
            wsl = b.qsw[:, par * 512:(par + 1) * 512]
            p.dma("ge", b.qsw[0:64, par * 512:(par + 1) * 512],
                  b.q_sb[64:128, hh * 512:(hh + 1) * 512],
                  wait=[("ve", q_evac_ve[min(hh + 1, 31)] if hh < 31
                         else q_evac_ve[31]),
                        ("ve", rope_hist[-2])], sem="dqs")
            cw = p.dma("ge", b.qsw[64:128, par * 512:(par + 1) * 512],
                       b.q_sb[0:64, hh * 512:(hh + 1) * 512], sem="dqs")
            swap_hist.append(cw)
            p.op("ve", lambda e, qsl=qsl, csl=csl: e.tensor_mul(
                b.rtq[:], qsl, csl),
                 wait=[("dqs", cw), ("dge", cqs)], inc=[("ve", 1)], dep=True)
            p.op("ve", lambda e, wsl=wsl, ssl=ssl: e.tensor_mul(
                wsl, wsl, ssl), inc=[("ve", 1)])
            p.op("ve", lambda e, qsl=qsl, wsl=wsl: e.tensor_add(
                qsl, b.rtq[:], wsl), inc=[("ve", 1)], dep=True)
            rope_hist.append(p.n("ve"))
        ve_q = p.n("ve")
        if DEBUG and g == 0:
            p.dma("ge", io["dbg_q"], b.q_sb[:], wait=[("ve", ve_q)], sem="dbg")
        if L1A_TRUNC < 4:
            continue

        for g2 in range(8):
            c_k = p.dma("sy", b.k_fm[:, 0:ext * 128],
                        io["k_stage"][g2 * 128:(g2 + 1) * 128, 0:ext * 128],
                        wait=[("te", p.n("te"))], sem="dkv")
            vsrc = bass.AP(io["v_stage"].tensor, g2 * 4096 * 128,
                           [[128, 128], [16384, ext], [1, 128]])
            c_v = p.dma("sy", b.v_tm[:, 0:ext * 128].rearrange(
                "p (jt c) -> p jt c", c=128), vsrc, sem="dkv")
            for h in range(4 * g2, 4 * g2 + 4):
                sex_hist = []
                av_hist = []
                ve_tail = p.n("ve")
                for jj in range(ext):
                    jt = jj
                    par = jj % 2
                    last = jj == ext - 1
                    if jj >= 2:
                        w_s = [("se", sex_hist[jj - 2])]
                    else:
                        w_s = [("ve", ve_q), ("dkv", c_k), ("se", last_exp_se)]
                    p.op("te", lambda e, jt=jt, h=h, par=par: e.matmul(
                        b.s_ps[par][:], b.k_fm[:, jt * 128:(jt + 1) * 128],
                        b.q_sb[:, h * 512:(h + 1) * 512], start=True, stop=True),
                         wait=w_s, inc=[("te", 1)])
                    te_s = p.n("te")
                    ptp = b.pt[par]
                    w_e = [("te", te_s), ("dsy", c_b1)]
                    if jj >= 2:
                        w_e.append(("te", av_hist[jj - 2]))
                    else:
                        w_e.append(("ve", ve_tail))
                    p.op("se", lambda e, ptp=ptp, g=g, jt=jt, par=par:
                         e.activation(ptp[:], b.s_ps[par][:], AF.Exp,
                                      bias=b.bias_sb[:, g * 32 + jt:
                                                     g * 32 + jt + 1],
                                      scale=SCALE),
                         wait=w_e, inc=[("se", 1)])
                    sex_hist.append(p.n("se"))
                    last_exp_se = sex_hist[-1]
                    slot = jt - (ext - 8)
                    if slot >= 0:
                        p.op("ge", lambda e, ptp=ptp, slot=slot: e.tensor_mul(
                            ptp[:], ptp[:],
                            b.mask_sb[:, slot * 512:(slot + 1) * 512]),
                             wait=[("se", sex_hist[-1]), ("dsy", c_m)],
                             inc=[("ge", 1)])
                        last_mask_ge = p.n("ge")
                        dep = [("ge", last_mask_ge)]
                    else:
                        dep = [("se", sex_hist[-1])]
                    p.op("te", lambda e, ptp=ptp, first=(jj == 0), last=last:
                         e.matmul(b.l_ps[:], b.ones_bf[:, 0:1], ptp[:],
                                  start=first, stop=last),
                         wait=dep, inc=[("te", 1)])
                    p.op("te", lambda e, ptp=ptp, jt=jt, first=(jj == 0),
                         last=last: e.matmul(
                             b.av_ps[:], b.v_tm[:, jt * 128:(jt + 1) * 128],
                             ptp[:], start=first, stop=last),
                         wait=([("dkv", c_v)] if jj == 0 else ()),
                         inc=[("te", 1)])
                    av_hist.append(p.n("te"))
                te_done = p.n("te")
                p.op("ve", lambda e: e.tensor_scalar(
                    b.l_sb, b.l_ps[:], 1e-35, None, ALU.add),
                     wait=[("te", te_done)], inc=[("ve", 1)])
                p.op("ve", lambda e: e.reciprocal(b.rl_sb, b.l_sb),
                     inc=[("ve", 1)], dep=True)
                ve_rl = p.n("ve")
                p.op("te", lambda e: e.matmul(
                    b.bc_ps[:], b.ones_f[0:1, :], b.rl_sb,
                    start=True, stop=True),
                     wait=[("ve", ve_rl)], inc=[("te", 1)])
                te_bc = p.n("te")
                p.op("ve", lambda e: e.tensor_copy(b.bc_sb[:], b.bc_ps[:]),
                     wait=[("te", te_bc)], inc=[("ve", 1)])
                p.op("ve", lambda e, h=h: e.scalar_tensor_tensor(
                    b.attn_sb[:, h * 512:(h + 1) * 512], b.av_ps[:], 1.0,
                    b.bc_sb[:], ALU.mult, ALU.mult),
                     inc=[("ve", 1)], dep=True)
        ve_attn = p.n("ve")
        if DEBUG and g == 0:
            p.dma("ge", io["dbg_attn"], b.attn_sb[:], wait=[("ve", ve_attn)],
                  sem="dbg")

        y_evac_ve = [0, 0]
        for ts in range(4):
            t_glob = 4 * g + ts
            for os_ in range(8):
                batch_hist = []
                cws = []
                for k in range(2):
                    base = (4 * k * 128) * 4096 + os_ * 512
                    srcw = bass.AP(io["wo_scr"].tensor, base,
                                   [[4096, 128], [524288, 4], [1, 512]])
                    w_b = getattr(b, "wob_free_te", [0, 0])[k]
                    cw = p.dma("se", b.wob[k][:].rearrange(
                        "p (dj c) -> p dj c", c=512), srcw,
                        wait=[("te", w_b)], sem="dwo")
                    cws.append(cw)
                for dc in range(32):
                    kb = dc // 4
                    if dc % 4 == 0 and kb >= 2:
                        base = (4 * kb * 128) * 4096 + os_ * 512
                        srcw = bass.AP(io["wo_scr"].tensor, base,
                                       [[4096, 128], [524288, 4], [1, 512]])
                        cw = p.dma("se", b.wob[kb % 2][:].rearrange(
                            "p (dj c) -> p dj c", c=512), srcw,
                            wait=[("te", batch_hist[kb - 2])], sem="dwo")
                        cws.append(cw)
                    w_y = [("dwo", cws[kb])]
                    if dc == 0:
                        w_y += [("ve", ve_attn), ("ve", y_evac_ve[-2])]
                    p.op("te", lambda e, dc=dc, ts=ts, par=(ts * 8 + os_) % 2:
                         e.matmul(
                             b.y_ps[par][:],
                             b.attn_sb[:, dc * 512 + ts * 128:
                                       dc * 512 + (ts + 1) * 128],
                             b.wob[(dc // 4) % 2][:, (dc % 4) * 512:
                                                  (dc % 4 + 1) * 512],
                             start=(dc == 0), stop=(dc == 31)),
                         wait=w_y, inc=[("te", 1)])
                    if dc % 4 == 3:
                        batch_hist.append(p.n("te"))
                b.wob_free_te = [batch_hist[-2], batch_hist[-1]]
                te_y = p.n("te")
                p.op("ve", lambda e, os_=os_, par=(ts * 8 + os_) % 2:
                     e.tensor_copy(b.y_sb[:, os_ * 512:(os_ + 1) * 512],
                                   b.y_ps[par][:]),
                     wait=[("te", te_y), ("se", getattr(b, "wsq_se", 0))],
                     inc=[("ve", 1)])
                y_evac_ve.append(p.n("ve"))
                ve_yc = p.n("ve")
                p.op("ve", lambda e, os_=os_: e.reduce_sum(
                    b.ys_cols[:, os_:os_ + 1],
                    b.y_sb[:, os_ * 512:(os_ + 1) * 512], axis=AX.X),
                     inc=[("ve", 1)], dep=True)
                p.op("se", lambda e, os_=os_: e.activation(
                    b.sqd[:], b.y_sb[:, os_ * 512:(os_ + 1) * 512],
                    AF.Square, accum_out=b.ysq_cols[:, os_:os_ + 1]),
                     wait=[("ve", ve_yc)], inc=[("se", 1)])
                b.wsq_se = p.n("se")
            emit_out_ln_residual(p, b, io["x_in"], io["x_out"], t_glob)
        phase_mark = p.n("ve")
    if L1A_TRUNC < 4:
        p.dma("sy", io["x_out"], io["x_in"], wait=[("ve", p.n("ve"))], sem="dxo")
    if DEBUG:
        p.dma("ge", io["dbg_ks"], io["k_stage"], sem="dbg")
        p.dma("ge", io["dbg_vs"], io["v_stage"], sem="dbg")
        p.op("ge", lambda e: e.nop(), wait=[("dbg", p.n("dbg"))])
    p.op("sy", lambda e: e.nop(), wait=[("dxo", p.n("dxo"))])


def alloc_l1a_bufs(p):
    nc = p.nc
    b = Bufs()
    b.id_f = p.sb("id_f", [128, 128], F32)
    b.id_bf = p.sb("id_bf", [128, 128], BF16)
    b.ones_f = p.sb("ones_f", [128, 128], F32)
    b.ones_bf = p.sb("ones_bf", [128, 128], BF16)
    b.scat = p.sb("scat", [128, NT], I32)
    b.xt = p.sb("xt", [128, 4096], F32)
    b.xt2 = p.sb("xt2", [128, 4096], F32)
    stage_v = nc.virtual_sbuf_tensor("stage_v", [128, 4096], BF16)
    nc.back_virtual_tensor_with(stage_v, b.xt2)
    b.stage = stage_v
    b.h_tm = p.sb("h_tm", [128, 4096], BF16)
    b.h_fm = p.sb("h_fm", [128, 4096], BF16)
    b.big = p.sb("big", [128, 49152], BF16)
    b.wcat = b.big[:, 0:32768]
    b.c_fm = b.big[:, 0:16384]
    b.kr_fm = b.big[:, 16384:32768]
    b.kcos = b.big[:, 32768:36864]
    b.ksin = b.big[:, 36864:40960]
    b.rt1 = b.big[:, 40960:45056]
    b.rt2 = b.big[:, 45056:49152]
    b.kc_bf = b.big[:, 32768:36864]
    b.q_sb = b.big[:, 0:16384]
    b.h_grp = b.big[:, 16384:32768]
    b.attn_sb = b.h_grp
    b.cos_q = b.big[:, 32768:34816]
    b.sin_qs = b.big[:, 34816:36864]
    b.qsw = b.big[:, 40960:45056][:, 0:16384] if False else None
    b.rtq = b.big[:, 45056:45568]
    b.vv_bf = b.big[:, 45568:45696]
    b.k_fm = p.sb("k_fm", [128, 4096], BF16)
    b.v_tm = p.sb("v_tm", [128, 4096], BF16)
    b.wupk = b.k_fm[:, 0:2048]
    b.wupv = b.v_tm[:]
    b.krc_bf = p.sb("krc_bf", [128, 1024], BF16)
    b.wblk = [p.sb(f"wblk{i}", [128, 4096], BF16) for i in range(2)]
    b.wob = [b.wblk[0][:, 0:2048], b.wblk[0][:, 2048:4096]]
    b.qsw = p.sb("qsw", [128, 1024], BF16)
    b.bias_sb = p.sb("bias_sb", [128, 128], F32)
    b.mask_sb = b.big[:, 40960:45056]
    b.pt = [p.sb(f"pt{i}", [128, 512], BF16) for i in range(2)]
    b.lrs = p.sb("lrs", [128, 1024], F32)
    b.y_sb = p.sb("y_sb", [128, 4096], F32)
    b.sqd = b.lrs[:, 0:512]
    b.l_sb = b.lrs[0:1, 0:512]
    b.rl_sb = b.lrs[0:1, 512:1024]
    b.bc_sb = p.sb("bc_sb", [128, 512], F32)
    b.ys_cols = p.sb("ys_cols", [128, 8], F32)
    b.ysq_cols = p.sb("ysq_cols", [128, 8], F32)
    for nme in ["s_sum", "s_ssq", "s_m", "s_msq", "s_v", "s_sd", "s_r"]:
        setattr(b, nme, p.sb(nme, [128, 1], F32))
    b.s_ps = [p.ps("s_ps0", [128, 512]), p.ps("s_ps1", [128, 512])]
    b.tp_ps = [p.ps("tp0", [128, 512], BF16), p.ps("tp1", [128, 512], BF16)]
    b.av_ps = p.ps("av_ps", [128, 512])
    b.lbc_ps = p.ps("lbc_ps", [128, 512])
    b.l_ps = b.lbc_ps[0:1, :]
    b.bc_ps = b.lbc_ps
    b.q_ps = [p.ps("qy_ps0", [128, 512]), p.ps("qy_ps1", [128, 512])]
    b.y_ps = b.q_ps
    return b


def build_launch_l1a():
    nc = bass.Bass("TRN2", target_bir_lowering=False, num_devices=8)
    ctx = ExitStack()
    io = {}
    io["x_in"] = nc.dram_tensor("x_in", [RPC, D], F32, kind="ExternalInput").ap()
    io["wq_t"] = nc.dram_tensor("wq_t", [D, D], F32, kind="ExternalInput").ap()
    io["wkrc"] = nc.dram_tensor("wkrc", [D, 1024], F32, kind="ExternalInput").ap()
    io["wupk_t"] = nc.dram_tensor("wupk_t", [512, 512], F32, kind="ExternalInput").ap()
    io["wupv_t"] = nc.dram_tensor("wupv_t", [512, 1024], F32, kind="ExternalInput").ap()
    io["wo_t"] = nc.dram_tensor("wo_t", [D, D], F32, kind="ExternalInput").ap()
    io["ident"] = nc.dram_tensor("ident", [128, 128], F32, kind="ExternalInput").ap()
    io["ones_f"] = nc.dram_tensor("ones_f", [128, 128], F32, kind="ExternalInput").ap()
    io["scat_idx"] = nc.dram_tensor("scat_idx", [128, NT], I32, kind="ExternalInput").ap()
    io["masks"] = nc.dram_tensor("masks", [128, 4 * 8 * 512], BF16, kind="ExternalInput").ap()
    io["bias1"] = nc.dram_tensor("bias1", [128, 128], F32, kind="ExternalInput").ap()
    io["cos_q"] = nc.dram_tensor("cos_q", [128, 2048], F32, kind="ExternalInput").ap()
    io["sin_qs"] = nc.dram_tensor("sin_qs", [128, 2048], F32, kind="ExternalInput").ap()
    io["cos_k"] = nc.dram_tensor("cos_k", [256, 4096], F32, kind="ExternalInput").ap()
    io["sin_k"] = nc.dram_tensor("sin_k", [256, 4096], F32, kind="ExternalInput").ap()
    io["x_out"] = nc.dram_tensor("x_out", [RPC, D], F32, kind="ExternalOutput").ap()
    io["xch1"] = nc.dram_tensor("xch1", [S, 1024], BF16, kind="Internal",
                                addr_space="Shared").ap()
    io["h_scr"] = nc.dram_tensor("h_scr", [RPC, D], BF16, kind="Internal").ap()
    io["wq_scr"] = nc.dram_tensor("wq_scr", [D * D], BF16, kind="Internal").ap()
    io["wo_scr"] = nc.dram_tensor("wo_scr", [D, D], BF16, kind="Internal").ap()
    io["k_stage"] = nc.dram_tensor("k_stage", [1024, 4096], BF16, kind="Internal").ap()
    if DEBUG:
        io["dbg_ks"] = nc.dram_tensor("dbg_ks", [1024, 4096], BF16, kind="ExternalOutput").ap()
        io["dbg_vs"] = nc.dram_tensor("dbg_vs", [8 * 4096, 128], BF16, kind="ExternalOutput").ap()
        io["dbg_q"] = nc.dram_tensor("dbg_q", [128, 32 * 512], BF16, kind="ExternalOutput").ap()
        io["dbg_attn"] = nc.dram_tensor("dbg_attn", [128, 32 * 512], BF16, kind="ExternalOutput").ap()
    io["v_stage"] = nc.dram_tensor("v_stage", [8 * 4096, 128], BF16, kind="Internal").ap()
    with ctx:
        p = Pr(nc, ctx)
        b = alloc_l1a_bufs(p)
        build_l1_attn(p, b, io)
        p.build()
    return nc


def l1a_in_map(inp, c, consts):
    perm = np.zeros(512, np.int64)
    for j in range(512):
        if j < 256:
            g, t = j // 32, j % 32
        else:
            g, t = (j - 256) // 32, 32 + (j - 256) % 32
        perm[j] = g * 64 + t
    wkr_pk = inp["L1_attn_Wkr"].T[:, perm]
    wkrc = np.ascontiguousarray(
        np.concatenate([wkr_pk, inp["L1_attn_Wdk"].T], axis=1))
    return dict(
        wq_t=np.ascontiguousarray(inp["L1_attn_Wq"].T),
        wkrc=wkrc,
        wupk_t=np.ascontiguousarray(inp["L1_attn_Wupk"].T),
        wupv_t=np.ascontiguousarray(inp["L1_attn_Wupv"].T),
        wo_t=np.ascontiguousarray(inp["L1_attn_Wo"].T),
        ident=consts["ident"], ones_f=consts["ones_f"],
        scat_idx=consts["scat_idx"], masks=consts["masks"],
        bias1=consts["bias1"], cos_q=consts["cos_q"],
        sin_qs=consts["sin_qs"], cos_k=consts["cos_k"],
        sin_k=consts["sin_k"])


def build_l1_ffn(p, b, io):
    nc = p.nc
    c0 = p.dma("sy", b.id_f[:], io["ident"], sem="dsy")
    c1 = p.dma("sy", b.ones_f[:], io["ones_f"], sem="dsy")
    p.op("ve", lambda e: e.tensor_copy(b.id_bf[:], b.id_f[:]),
         wait=[("dsy", c1)], inc=[("ve", 1)])
    NW = 8 * 512 * 1366
    weights = [("wg_t", io["wg_scr"], "gu"), ("wu_t", io["wu_scr"], "gu"),
               ("wd_t", io["wd_scr"], "d")]
    sc_bcs = []
    p.op("ve", lambda e: e.memset(b.acc[:], 0.0), inc=[("ve", 1)])
    for wi, (wname, _scr, kind) in enumerate(weights):
        wt = io[wname]
        nacc = wi * 36 if wi < 2 else 72
        if kind == "gu":
            chunks = [(j, ds, 128, wt[j * 512 + ds * 128:
                                      j * 512 + (ds + 1) * 128, :])
                      for j in range(8) for ds in range(4)]
            ncols = 1366
        else:
            chunks = [(j, ds, (86 if ds == 10 else 128),
                       wt[j * 1366 + ds * 128:
                          j * 1366 + ds * 128 + (86 if ds == 10 else 128), :])
                      for j in range(8) for ds in range(11)]
            ncols = 512
        for (j, ds, nrow, csrc) in chunks:
            cw = p.dma("sy", b.wload[0:nrow, 0:ncols], csrc,
                       wait=[("se", p.n("se"))], sem="dwl")
            p.op("se", lambda e, nrow=nrow, ncols=ncols, nacc=nacc: e.activation(
                b.sqd[0:nrow, 0:ncols] if ncols <= 512 else
                b.wdump[0:nrow, 0:ncols],
                b.wload[0:nrow, 0:ncols], AF.Abs,
                accum_out=b.acc[0:nrow, nacc:nacc + 1]),
                 wait=[("dwl", cw)], inc=[("se", 1)])
            nacc += 1
        n0 = wi * 36 if wi < 2 else 72
        p.op("ve", lambda e, n0=n0, nacc=nacc: e.reduce_sum(
            b.s_sum[:], b.acc[:, n0:nacc], axis=AX.X),
             wait=[("se", p.n("se")), ("te", p.n("te"))],
             inc=[("ve", 1)], dep=True)
        ve_r = p.n("ve")
        p.op("te", lambda e, wi=wi: e.matmul(
            b.sc_ps[0:1, 100 + wi:101 + wi], b.s_sum[:],
            b.ones_f[:, 0:1], start=True, stop=True),
             wait=[("ve", ve_r), ("ve", p.n("ve"))], inc=[("te", 1)])
        te_s = p.n("te")
        p.op("ve", lambda e, wi=wi: e.tensor_scalar_mul(
            b.sc_row[0:1, wi * 3:wi * 3 + 1],
            b.sc_ps[0:1, 100 + wi:101 + wi], 1.0 / NW),
             wait=[("te", te_s)], inc=[("ve", 1)])
        p.op("ve", lambda e, wi=wi: e.tensor_scalar_mul(
            b.sc_row[0:1, wi * 3 + 1:wi * 3 + 2],
            b.sc_row[0:1, wi * 3:wi * 3 + 1], 0.5),
             inc=[("ve", 1)], dep=True)
        p.op("ve", lambda e, wi=wi: e.tensor_mul(
            b.sc_row[0:1, wi * 3 + 2:wi * 3 + 3],
            b.sc_row[0:1, wi * 3:wi * 3 + 1],
            b.sc_row[0:1, wi * 3:wi * 3 + 1]),
             inc=[("ve", 1)], dep=True)
        ve_sc = p.n("ve")
        p.op("te", lambda e, wi=wi: e.matmul(
            b.sc_ps[:, wi * 3:wi * 3 + 3], b.ones_f[0:1, :],
            b.sc_row[0:1, wi * 3:wi * 3 + 3], start=True, stop=True),
             wait=[("ve", ve_sc)], inc=[("te", 1)])
        te_b = p.n("te")
        sbc = b.scb[:, wi * 4:wi * 4 + 1]
        tbc = b.scb[:, wi * 4 + 1:wi * 4 + 2]
        s2bc = b.scb[:, wi * 4 + 2:wi * 4 + 3]
        ntbc = b.scb[:, wi * 4 + 3:wi * 4 + 4]
        p.op("ve", lambda e, wi=wi: e.tensor_copy(
            b.scb[:, wi * 4:wi * 4 + 3], b.sc_ps[:, wi * 3:wi * 3 + 3]),
             wait=[("te", te_b)], inc=[("ve", 1)])
        p.op("ve", lambda e, tbc=tbc, ntbc=ntbc: e.tensor_scalar_mul(
            ntbc, tbc, -1.0), inc=[("ve", 1)], dep=True)
        sc_bcs.append((sbc, tbc, s2bc, ntbc))

    p.op("ve", lambda e: e.memset(b.plus[:], 0.0), inc=[("ve", 1)])
    p.op("ve", lambda e: e.memset(b.minus[:], 0.0), inc=[("ve", 1)])
    dq_ve = p.n("ve")
    for wi, (wname, scr, kind) in enumerate(weights):
        wt = io[wname]
        _s, tbc, _s2, ntbc = sc_bcs[wi]
        if kind == "gu":
            specs = [(j, ds, 128, 1366, 1408) for j in range(8)
                     for ds in range(4)]
        else:
            specs = [(j, ds, (86 if ds == 10 else 128), 512, 512)
                     for j in range(8) for ds in range(11)]
        for (j, ds, nrow, ncols, ncp) in specs:
            if kind == "gu":
                csrc = wt[j * 512 + ds * 128:j * 512 + (ds + 1) * 128, :]
            else:
                csrc = wt[j * 1366 + ds * 128:j * 1366 + ds * 128 + nrow, :]
            cw = p.dma("sy", b.wload[0:nrow, 0:ncols], csrc,
                       wait=[("ve", dq_ve)], sem="dwl")
            if nrow < 128:
                p.op("ve", lambda e: e.memset(b.plus[:, 0:512], 0.0),
                     wait=[("wsc", p.n("wsc"))], inc=[("ve", 1)])
                p.op("ve", lambda e: e.memset(b.minus[:, 0:512], 0.0),
                     inc=[("ve", 1)])
            p.op("ve", lambda e, nrow=nrow, ncols=ncols, tbc=tbc: e.tensor_scalar(
                b.pp[0:nrow, 0:ncols], b.wload[0:nrow, 0:ncols],
                tbc[0:nrow, :], None, ALU.is_gt),
                 wait=[("dwl", cw), ("wsc", p.n("wsc"))], inc=[("ve", 1)])
            p.op("ve", lambda e, nrow=nrow, ncols=ncols, ntbc=ntbc:
                 e.tensor_scalar(
                     b.nn[0:nrow, 0:ncols], b.wload[0:nrow, 0:ncols],
                     ntbc[0:nrow, :], None, ALU.is_lt),
                 inc=[("ve", 1)])
            p.op("ve", lambda e, nrow=nrow, ncols=ncols: e.tensor_sub(
                b.plus[0:nrow, 0:ncols], b.pp[0:nrow, 0:ncols],
                b.nn[0:nrow, 0:ncols]),
                 inc=[("ve", 1)], dep=True)
            p.op("ve", lambda e, nrow=nrow, ncols=ncols: e.tensor_sub(
                b.minus[0:nrow, 0:ncols], b.nn[0:nrow, 0:ncols],
                b.pp[0:nrow, 0:ncols]),
                 inc=[("ve", 1)], dep=True)
            dq_ve = p.n("ve")
            for i in range(8):
                k = i ^ j
                sgn = int(OCT_G[i, k])
                srcb = b.plus if sgn > 0 else b.minus
                if kind == "gu":
                    basei = (k * 11 * 32 + i * 4 + ds) * 16384
                    dst = bass.AP(scr.tensor, basei,
                                  [[128, 128], [32 * 16384, 11], [1, 128]])
                    p.dma("ge", dst, srcb[:, 0:1408].rearrange(
                        "p (os c) -> p os c", c=128),
                        wait=[("ve", dq_ve)], sem="wsc")
                else:
                    basei = (k * 4 * 88 + i * 11 + ds) * 16384
                    dst = bass.AP(scr.tensor, basei,
                                  [[128, 128], [88 * 16384, 4], [1, 128]])
                    p.dma("ge", dst, srcb[:, 0:512].rearrange(
                        "p (os c) -> p os c", c=128),
                        wait=[("ve", dq_ve)], sem="wsc")
    c_dq = p.n("wsc")
    b.ysb_free_ve = p.n("ve")

    phase_mark = 0
    sgbc = sc_bcs[0][0]
    subc = sc_bcs[1][0]
    sdbc = sc_bcs[2][0]
    sd2bc = sc_bcs[2][2]
    for tc in range(4):
        for tt in range(4):
            b.h_fm = b.hfm4[:, tt * 4096:(tt + 1) * 4096]
            ve_h = emit_input_ln(p, b, io["x_in"], 4 * tc + tt)
            emit_transpose_tile(p, b, ve_h, extra_wait=[("ve", phase_mark)])
        ve_hf = p.n("ve")
        mid_ve = 0
        gu_te = [0, 0, 0, 0]
        for os_ in range(11):
            for k in range(8):
                par = (os_ * 8 + k) % 2
                cg = p.dma("sy", b.gblk[:],
                           bass.AP(io["wg_scr"].tensor,
                                   (k * 11 + os_) * 32 * 16384,
                                   [[128, 128], [16384, 32], [1, 128]]),
                           wait=[("te", gu_te[-2]), ("ve", phase_mark),
                                 ("wsc", c_dq)],
                           sem="dgb")
                cu = p.dma("se", b.ublk[:],
                           bass.AP(io["wu_scr"].tensor,
                                   (k * 11 + os_) * 32 * 16384,
                                   [[128, 128], [16384, 32], [1, 128]]),
                           wait=[("te", gu_te[-1])], sem="dub")
                for ci in range(32):
                    rhs = b.hfm4[:, 0:16384].rearrange(
                        "p (tt c) -> p tt c", c=4096)[:, :,
                                                      ci * 128:(ci + 1) * 128]
                    p.op("te", lambda e, ci=ci, rhs=rhs: e.matmul(
                        b.g_ps[:].rearrange("p (tt c) -> p tt c", c=128),
                        b.gblk[:, ci * 128:(ci + 1) * 128], rhs,
                        start=(ci == 0), stop=(ci == 31)),
                         wait=([("dgb", cg), ("ve", ve_hf),
                                ("se", getattr(b, "silu_se", 0))]
                               if ci == 0 else ()),
                         inc=([("te", 1)] if ci == 31 else ()))
                gu_te.append(p.n("te"))
                for ci in range(32):
                    rhs = b.hfm4[:, 0:16384].rearrange(
                        "p (tt c) -> p tt c", c=4096)[:, :,
                                                      ci * 128:(ci + 1) * 128]
                    p.op("te", lambda e, ci=ci, rhs=rhs: e.matmul(
                        b.u_ps[:].rearrange("p (tt c) -> p tt c", c=128),
                        b.ublk[:, ci * 128:(ci + 1) * 128], rhs,
                        start=(ci == 0), stop=(ci == 31)),
                         wait=([("dub", cu), ("ve", mid_ve)]
                               if ci == 0 else ()),
                         inc=([("te", 1)] if ci == 31 else ()))
                gu_te.append(p.n("te"))
                te_gu = p.n("te")
                p.op("se", lambda e, sgbc=sgbc: e.activation(
                    b.silu_sb[:], b.g_ps[:], AF.Silu, scale=sgbc),
                     wait=[("te", te_gu)], inc=[("se", 1)])
                b.silu_se = p.n("se")
                p.op("ve", lambda e, k=k, os_=os_, subc=subc:
                     e.scalar_tensor_tensor(
                         b.h_mid[:, (k * 11 + os_) * 512:
                                 (k * 11 + os_ + 1) * 512],
                         b.u_ps[:], subc, b.silu_sb[:], ALU.mult, ALU.mult),
                     wait=[("se", b.silu_se)], inc=[("ve", 1)])
                mid_ve = p.n("ve")
        dn_te = [0, 0]
        ydf_ve = [0, 0]
        for oc2 in range(32):
            par = oc2 % 2
            for hf in range(2):
                cd = p.dma("sy", b.dblk[:],
                           bass.AP(io["wd_scr"].tensor,
                                   (oc2 * 88 + hf * 44) * 16384,
                                   [[128, 128], [16384, 44], [1, 128]]),
                           wait=[("te", dn_te[-1] if hf == 0 else p.n("te")),
                                 ("ve", mid_ve), ("wsc", c_dq)], sem="ddb")
                order = list(range(44)) if hf == 0 else                     [43] + list(range(43))
                for k3, cc in enumerate(order):
                    c = hf * 44 + cc
                    first = (hf == 0 and k3 == 0)
                    lastm = (hf == 1 and k3 == 43)
                    p.op("te", lambda e, c=c, cc=cc, par=par, first=first,
                         lastm=lastm: e.matmul(
                             b.y2_ps[par][:], b.dblk[:, cc * 128:(cc + 1) * 128],
                             b.h_mid[:, c * 512:(c + 1) * 512],
                             start=first, stop=lastm),
                         wait=([("ddb", cd), ("ve", ydf_ve[-2])]
                               if k3 == 0 else ()),
                         inc=([("te", 1)] if k3 == 43 else ()))
            dn_te.append(p.n("te"))
            p.op("ve", lambda e, oc2=oc2, par=par: e.tensor_copy(
                b.ydf[:, oc2 * 512:(oc2 + 1) * 512], b.y2_ps[par][:]),
                 wait=[("te", dn_te[-1])], inc=[("ve", 1)])
            ydf_ve.append(p.n("ve"))
        ve_dn = p.n("ve")
        for ts in range(4):
            tr_hist = getattr(b, "ytr_hist", [0, 0])
            for grp in range(8):
                parg = grp % 2
                tp = b.tp_ps[parg]
                for cc in range(4):
                    oc2 = grp * 4 + cc
                    p.op("te", lambda e, oc2=oc2, cc=cc, tp=tp, ts=ts:
                         e.transpose(tp[:, cc * 128:(cc + 1) * 128],
                                     b.ydf[:, oc2 * 512 + ts * 128:
                                           oc2 * 512 + (ts + 1) * 128],
                                     b.id_bf[:]),
                         wait=([("ve", ve_dn), ("ve", tr_hist[-2])]
                               if cc == 0 else ()),
                         inc=([("te", 1)] if cc == 3 else ()))
                te_c = p.n("te")
                p.op("ve", lambda e, grp=grp, tp=tp: e.tensor_copy(
                    b.y_sb[:, grp * 512:(grp + 1) * 512], tp[:]),
                     wait=[("te", te_c)], inc=[("ve", 1)])
                tr_hist.append(p.n("ve"))
            b.ytr_hist = tr_hist[-2:]
            ve_y = p.n("ve")
            emit_out_ln_residual(p, b, io["x_in"], io["x_out"], 4 * tc + ts,
                                 fold=(sdbc, sd2bc), direct=True,
                                 y_wait=[("ve", ve_y)])
        phase_mark = p.n("ve")
        if DEBUG and tc == 3:
            p.dma("ge", io["dbg_wd"],
                  bass.AP(io["wd_scr"].tensor, 0, [[128, 88 * 128], [1, 128]]),
                  wait=[("wsc", c_dq)], sem="dbg")
            p.dma("ge", io["dbg_mid"], b.h_mid[:], wait=[("ve", mid_ve)], sem="dbg")
            p.dma("ge", io["dbg_ydf"], b.ydf[:], wait=[("ve", ve_dn)], sem="dbg")
            p.dma("ge", io["dbg_scb"], b.scb[:], sem="dbg")
            p.op("ge", lambda e: e.nop(), wait=[("dbg", p.n("dbg"))])
    p.op("sy", lambda e: e.nop(), wait=[("dxo", p.n("dxo"))])


def alloc_l1f_bufs(p):
    nc = p.nc
    b = Bufs()
    b.id_f = p.sb("id_f", [128, 128], F32)
    b.id_bf = p.sb("id_bf", [128, 128], BF16)
    b.ones_f = p.sb("ones_f", [128, 128], F32)
    b.xt = p.sb("xt", [128, 4096], F32)
    b.xt2 = p.sb("xt2", [128, 4096], F32)
    b.y_sb = p.sb("y_sb", [128, 4096], F32)
    b.sqd2 = b.xt
    b.h_mid = p.sb("h_mid", [128, 88 * 512], BF16)
    b.hfm4 = p.sb("hfm4", [128, 4 * 4096], BF16)
    b.ydf = b.hfm4
    b.dblk = p.sb("dblk", [128, 44 * 128], BF16)
    b.h_tm = b.dblk[:, 0:4096]
    b.gblk = p.sb("gblk", [128, 4096], BF16)
    b.ublk = p.sb("ublk", [128, 4096], BF16)
    b.plus = b.h_mid[:, 0:1408]
    b.minus = b.h_mid[:, 1408:2816]
    b.pp = b.h_mid[:, 2816:4224]
    b.nn = b.h_mid[:, 4224:5632]
    b.wload = b.y_sb
    b.wdump = b.xt
    b.silu_sb = p.sb("silu_sb", [128, 512], F32)
    b.sqd = p.sb("sqd", [128, 512], F32)
    b.acc = p.sb("acc", [128, 160], F32)
    b.sc_row = p.sb("sc_row", [1, 16], F32)
    b.scb = p.sb("scb", [128, 12], F32)
    for nme in ["s_sum", "s_ssq", "s_m", "s_msq", "s_v", "s_sd", "s_r"]:
        setattr(b, nme, p.sb(nme, [128, 1], F32))
    b.tp_ps = [p.ps("tp0", [128, 512], BF16), p.ps("tp1", [128, 512], BF16)]
    b.g_ps = p.ps("g_ps", [128, 512])
    b.u_ps = p.ps("u_ps", [128, 512])
    b.y2_ps = [p.ps("y2_ps0", [128, 512]), p.ps("y2_ps1", [128, 512])]
    b.sc_ps = p.ps("sc_ps", [128, 512])
    return b


def build_launch_l1f():
    nc = bass.Bass("TRN2", target_bir_lowering=False, num_devices=8)
    ctx = ExitStack()
    io = {}
    io["x_in"] = nc.dram_tensor("x_in", [RPC, D], F32, kind="ExternalInput").ap()
    io["wg_t"] = nc.dram_tensor("wg_t", [8 * 512, 1366], F32, kind="ExternalInput").ap()
    io["wu_t"] = nc.dram_tensor("wu_t", [8 * 512, 1366], F32, kind="ExternalInput").ap()
    io["wd_t"] = nc.dram_tensor("wd_t", [8 * 1366, 512], F32, kind="ExternalInput").ap()
    io["ident"] = nc.dram_tensor("ident", [128, 128], F32, kind="ExternalInput").ap()
    io["ones_f"] = nc.dram_tensor("ones_f", [128, 128], F32, kind="ExternalInput").ap()
    io["x_out"] = nc.dram_tensor("x_out", [RPC, D], F32, kind="ExternalOutput").ap()
    io["wg_scr"] = nc.dram_tensor("wg_scr", [8 * 11 * 32 * 16384], BF16, kind="Internal").ap()
    io["wu_scr"] = nc.dram_tensor("wu_scr", [8 * 11 * 32 * 16384], BF16, kind="Internal").ap()
    io["wd_scr"] = nc.dram_tensor("wd_scr", [32 * 88 * 16384], BF16, kind="Internal").ap()
    if DEBUG:
        io["dbg_wd"] = nc.dram_tensor("dbg_wd", [88 * 128, 128], BF16, kind="ExternalOutput").ap()
        io["dbg_mid"] = nc.dram_tensor("dbg_mid", [128, 88 * 512], BF16, kind="ExternalOutput").ap()
        io["dbg_ydf"] = nc.dram_tensor("dbg_ydf", [128, 16384], BF16, kind="ExternalOutput").ap()
        io["dbg_scb"] = nc.dram_tensor("dbg_scb", [128, 12], F32, kind="ExternalOutput").ap()
    with ctx:
        p = Pr(nc, ctx)
        b = alloc_l1f_bufs(p)
        build_l1_ffn(p, b, io)
        p.build()
    return nc


def l1f_in_map(inp, c, consts):
    return dict(
        wg_t=np.ascontiguousarray(
            inp["L1_ffn_w_gate"].transpose(0, 2, 1).reshape(8 * 512, 1366)),
        wu_t=np.ascontiguousarray(
            inp["L1_ffn_w_up"].transpose(0, 2, 1).reshape(8 * 512, 1366)),
        wd_t=np.ascontiguousarray(
            inp["L1_ffn_w_down"].transpose(0, 2, 1).reshape(8 * 1366, 512)),
        ident=consts["ident"], ones_f=consts["ones_f"])


def alloc_l0_bufs(p):
    nc = p.nc
    b = Bufs()
    b.id_f = p.sb("id_f", [128, 128], F32)
    b.id_bf = p.sb("id_bf", [128, 128], BF16)
    b.ones_f = p.sb("ones_f", [128, 128], F32)
    b.ones_bf = p.sb("ones_bf", [128, 128], BF16)
    b.scat = p.sb("scat", [128, NT], I32)
    b.wkv = p.sb("wkv", [128, 32 * 256], BF16)
    b.xt = p.sb("xt", [128, 4096], F32)
    b.xt2 = p.sb("xt2", [128, 4096], F32)
    stage_v = nc.virtual_sbuf_tensor("stage_v", [128, 4096], BF16)
    nc.back_virtual_tensor_with(stage_v, b.xt2.tensor if hasattr(b.xt2, "tensor") else b.xt2)
    b.stage = stage_v
    b.h_tm = p.sb("h_tm", [128, 4096], BF16)
    b.h_fm = p.sb("h_fm", [128, 4096], BF16)
    b.h_grp = p.sb("h_grp", [128, 4 * 4096], BF16)
    b.attn_sb = b.h_grp
    b.q_sb = p.sb("q_sb", [128, 32 * 512], BF16)
    b.k_fm = p.sb("k_fm", [128, 4096], BF16)
    b.v_tm = p.sb("v_tm", [128, 4096], BF16)
    b.kv_bf = p.sb("kv_bf", [128, 256], BF16)
    b.wblk = [p.sb(f"wblk{i}", [128, 4096], BF16) for i in range(2)]
    b.wob = [b.wblk[0][:, 0:2048], b.wblk[0][:, 2048:4096]]
    b.bias_sb = p.sb("bias_sb", [128, max(L0_NCALL)], F32)
    b.mask_sb = p.sb("mask_sb", [128, 8 * 512], BF16)
    b.pt = [p.sb(f"pt{i}", [128, 512], BF16) for i in range(2)]
    b.y_sb = p.sb("y_sb", [128, 4096], F32)
    b.sqd = p.sb("sqd", [128, 512], F32)
    b.bc_sb = p.sb("bc_sb", [128, 512], F32)
    b.l_sb = p.sb("l_sb", [1, 512], F32)
    b.rl_sb = p.sb("rl_sb", [1, 512], F32)
    b.ys_cols = p.sb("ys_cols", [128, 8], F32)
    b.ysq_cols = p.sb("ysq_cols", [128, 8], F32)
    for nme in ["s_sum", "s_ssq", "s_m", "s_msq", "s_v", "s_sd", "s_r"]:
        setattr(b, nme, p.sb(nme, [128, 1], F32))
    b.s_ps = [p.ps("s_ps0", [128, 512]), p.ps("s_ps1", [128, 512])]
    b.tp_ps = [p.ps("tp0", [128, 512], BF16), p.ps("tp1", [128, 512], BF16)]
    b.av_ps = p.ps("av_ps", [128, 512])
    b.kv_ps = b.av_ps[:, 0:256]
    b.lbc_ps = p.ps("lbc_ps", [128, 512])
    b.l_ps = b.lbc_ps[0:1, :]
    b.bc_ps = b.lbc_ps
    b.q_ps = [p.ps("qy_ps0", [128, 512]), p.ps("qy_ps1", [128, 512])]
    b.y_ps = b.q_ps
    return b


def build_launch_l0a():
    nc = bass.Bass("TRN2", target_bir_lowering=False, num_devices=8)
    ctx = ExitStack()
    io = {}
    io["x_in"] = nc.dram_tensor("x_in", [RPC, D], F32, kind="ExternalInput").ap()
    io["Wq"] = nc.dram_tensor("Wq", [D, D], F32, kind="ExternalInput").ap()
    io["Wkv"] = nc.dram_tensor("Wkv", [D, 256], F32, kind="ExternalInput").ap()
    io["Wo"] = nc.dram_tensor("Wo", [D, D], F32, kind="ExternalInput").ap()
    io["ident"] = nc.dram_tensor("ident", [128, 128], F32, kind="ExternalInput").ap()
    io["ones_f"] = nc.dram_tensor("ones_f", [128, 128], F32, kind="ExternalInput").ap()
    io["scat_idx"] = nc.dram_tensor("scat_idx", [128, NT], I32, kind="ExternalInput").ap()
    io["masks"] = nc.dram_tensor("masks", [128, 4 * 8 * 512], BF16, kind="ExternalInput").ap()
    io["bias0"] = nc.dram_tensor("bias0", [128, sum(L0_NCALL)], F32, kind="ExternalInput").ap()
    io["x_out"] = nc.dram_tensor("x_out", [RPC, D], F32, kind="ExternalOutput").ap()
    io["xch0"] = nc.dram_tensor("xch0", [S, 256], BF16, kind="Internal",
                                addr_space="Shared").ap()
    if DEBUG:
        io["dbg_h"] = nc.dram_tensor("dbg_h", [RPC, D], BF16, kind="ExternalOutput").ap()
        io["dbg_kv"] = nc.dram_tensor("dbg_kv", [S, 256], BF16, kind="ExternalOutput").ap()
        io["dbg_q"] = nc.dram_tensor("dbg_q", [128, 32 * 512], BF16, kind="ExternalOutput").ap()
        io["dbg_attn"] = nc.dram_tensor("dbg_attn", [128, 32 * 512], BF16, kind="ExternalOutput").ap()
        io["dbg_y"] = nc.dram_tensor("dbg_y", [128, 4096], F32, kind="ExternalOutput").ap()
        io["dbg_l"] = nc.dram_tensor("dbg_l", [1, 512], F32, kind="ExternalOutput").ap()
        io["dbg_rl"] = nc.dram_tensor("dbg_rl", [1, 512], F32, kind="ExternalOutput").ap()
        io["dbg_bc"] = nc.dram_tensor("dbg_bc", [128, 512], F32, kind="ExternalOutput").ap()
        io["dbg_pt"] = nc.dram_tensor("dbg_pt", [128, 512], BF16, kind="ExternalOutput").ap()
    io["h_scr"] = nc.dram_tensor("h_scr", [RPC, D], BF16, kind="Internal").ap()
    io["wq_scr"] = nc.dram_tensor("wq_scr", [D * D], BF16, kind="Internal").ap()
    io["wo_scr"] = nc.dram_tensor("wo_scr", [D, D], BF16, kind="Internal").ap()
    with ctx:
        p = Pr(nc, ctx)
        b = alloc_l0_bufs(p)
        build_l0_attn(p, b, io)
        p.build()
    return nc


def l0a_in_map(inp, c, consts):
    rows = own_rows(c % 2)
    x = np.ascontiguousarray(inp["L0_attn_hidden_states"][c // 2][rows])
    return dict(
        x_in=x,
        Wq=inp["L0_attn_Wq"], Wkv=inp["L0_attn_Wkv"], Wo=inp["L0_attn_Wo"],
        ident=consts["ident"], ones_f=consts["ones_f"],
        scat_idx=consts["scat_idx"], masks=consts["masks"],
        bias0=consts["bias0"])


def kernel(**inp):
    consts = [host_consts(c % 2) for c in range(8)]
    rows = [own_rows(c % 2) for c in range(8)]
    x = np.ascontiguousarray(inp["L0_attn_hidden_states"], dtype=np.float32)

    def shard(xa):
        return [np.ascontiguousarray(xa[c // 2][rows[c]]) for c in range(8)]

    def gather(res):
        out = np.zeros((B, S, D), dtype=np.float32)
        for c in range(8):
            out[c // 2][rows[c]] = res.results[c]["x_out"]
        return out

    nc = build_launch_l0a()
    xs = shard(x)
    maps = []
    for c in range(8):
        m = l0a_in_map(inp, c, consts[c])
        m["x_in"] = xs[c]
        maps.append(m)
    x = gather(run_bass_kernel_spmd(nc, maps, core_ids=list(range(8))))

    nc = build_launch_l0f()
    xs = shard(x)
    maps = []
    for c in range(8):
        m = l0f_in_map(inp, c, consts[c])
        m["x_in"] = xs[c]
        maps.append(m)
    x = gather(run_bass_kernel_spmd(nc, maps, core_ids=list(range(8))))

    nc = build_launch_l1a()
    xs = shard(x)
    maps = []
    for c in range(8):
        m = l1a_in_map(inp, c, consts[c])
        m["x_in"] = xs[c]
        maps.append(m)
    x = gather(run_bass_kernel_spmd(nc, maps, core_ids=list(range(8))))

    nc = build_launch_l1f()
    xs = shard(x)
    maps = []
    for c in range(8):
        m = l1f_in_map(inp, c, consts[c])
        m["x_in"] = xs[c]
        maps.append(m)
    x = gather(run_bass_kernel_spmd(nc, maps, core_ids=list(range(8))))
    return x
